# version 05 (final)
# baseline (speedup 1.0000x reference)
import numpy as np
import ml_dtypes
import concourse.bass as bass
import concourse.mybir as mybir
from concourse.bass_utils import run_bass_kernel_spmd
from concourse.engine_type import EngineType as ET

F32 = mybir.dt.float32
BF16 = mybir.dt.bfloat16
F8 = mybir.dt.float8e4
AF = mybir.ActivationFunctionType
ALU = mybir.AluOpType
AX = mybir.AxisListType

D = 2048
S = 2048
NB = 4
I0 = 6144
I1 = 8192
SCALE0 = float(128 ** -0.5)
TANH_SC = float((1.0 / np.sqrt(128.0)) / 30.0)
MAGIC = 12582912.0
EPS = 1e-5
T1 = 1280
PRE = 256

DEBUG_PHASE = 99
ATTN0_LIMIT = 99


def ts(i, n=128):
    return slice(i * n, (i + 1) * n)


class Role:
    def __init__(self, role):
        self.role = role
        if role == 0:
            self.kv_lo, self.kv_hi = 0, T1
            self.l0_lo, self.l0_hi = 0, T1
            self.pre_n = 0
            self.q_lo, self.q_hi = 0, T1
        else:
            self.kv_lo, self.kv_hi = 0, S
            self.l0_lo, self.l0_hi = PRE, S
            self.pre_n = PRE
            self.q_lo, self.q_hi = T1, S
        self.kv_n = self.kv_hi - self.kv_lo
        self.l0_n = self.l0_hi - self.l0_lo
        self.q_n = self.q_hi - self.q_lo
        self.NKV = self.kv_n // 128
        self.NT0 = self.l0_n // 128
        self.NQT = self.q_n // 128
        self.qt0 = self.q_lo // 128
        self.kt0 = self.l0_lo // 128
        self.NTK = self.NT0
        self.kv_groups = [(o, min(512, self.kv_n - o)) for o in range(0, self.kv_n, 512)]
        l0o = self.l0_lo - self.kv_lo
        self.l0_groups = [(l0o + o, min(512, self.l0_n - o)) for o in range(0, self.l0_n, 512)]
        self.qgroups = [(o, min(512, self.q_n - o)) for o in range(0, self.q_n, 512)]
        tc = 384
        self.f0_chunks = [(o, min(tc, self.l0_n - o)) for o in range(0, self.l0_n, tc)]
        self.f1_groups = [(0, 3), (3, 3)] if role else [(0, 4), (4, 3), (7, 3)]



ENGS = ("y", "g", "s", "v", "t")


class Op:
    __slots__ = ("eng", "fn", "deps", "sem", "k", "val", "need", "idx")

    def __init__(self, eng, fn, deps, sem, k, idx):
        self.eng, self.fn, self.deps, self.sem, self.k = eng, fn, deps, sem, k
        self.val = None
        self.need = False
        self.idx = idx


def _flat(deps):
    r = []
    for d in deps:
        if d is None:
            continue
        if isinstance(d, (list, tuple)) and not (len(d) == 2 and isinstance(d[0], str)):
            r.extend(_flat(d))
        else:
            r.append(d)
    return r


class Pipe:
    def __init__(self, C):
        self.C = C
        self.ops = []
        self.rr = {"y": 0, "s": 0, "g": 0}

    def op(self, eng, fn, deps=(), dma=False, ndma=1):
        deps = _flat(deps)
        if dma:
            pool = self.C["pools"][eng]
            sem = pool[self.rr[eng] % len(pool)]
            self.rr[eng] += 1
            sems = self.C["sems"]
            prev = sems[sem].n
            if prev:
                deps.append((sem, prev))
            o = Op(eng, fn, deps, sem, 16 * ndma, len(self.ops))
            o.need = True
            sems[sem].n += o.k
            o.val = sems[sem].n
        else:
            sem = {"t": "sPE", "v": "sV", "s": "sS", "g": "sGC"}[eng]
            o = Op(eng, fn, deps, sem, 1, len(self.ops))
        self.ops.append(o)
        return o

    def emit(self):
        C = self.C
        nc = C["nc"]
        sems = C["sems"]
        for o in self.ops:
            for d in o.deps:
                if isinstance(d, Op):
                    assert d.idx < o.idx, "dep ordering violated"
                    if d.eng != o.eng or d.k == 16:
                        d.need = True
        ctr = {k: v.n for k, v in sems.items()}
        for o in self.ops:
            if o.need and o.val is None:
                ctr[o.sem] += o.k
                o.val = ctr[o.sem]
        engmap = {"t": "tensor", "v": "vector", "s": "scalar", "g": "gpsimd", "y": "sync"}

        def mk_section(eng):
            ops = [o for o in self.ops if o.eng == eng]

            def section(e):
                waited = {}
                for o in ops:
                    for d in o.deps:
                        if isinstance(d, Op):
                            if d.eng == o.eng and d.k == 1:
                                continue
                            dsem, dval = d.sem, d.val
                            assert dval is not None, "dep never marked"

                        else:
                            dsem, dval = d
                        if dval > waited.get(dsem, 0):
                            e.wait_ge(sems[dsem].h, dval)
                            waited[dsem] = dval
                    ins = o.fn(e)
                    if o.val is not None:
                        if isinstance(ins, list):
                            assert o.k == 16 * len(ins), (o.k, len(ins))
                            for q in ins:
                                q.then_inc(sems[o.sem].h, 16)
                        else:
                            ins.then_inc(sems[o.sem].h, o.k)
            return section

        with nc.Block() as blk:
            for eng in ENGS:
                getattr(blk, engmap[eng])(mk_section(eng))
        for k, v in ctr.items():
            sems[k].n = v


class Sem:
    def __init__(self, handle, name):
        self.h = handle
        self.name = name
        self.n = 0



def build_program():
    nc = bass.Bass("TRN2", target_bir_lowering=False, detect_race_conditions=False)

    ap = {}
    def ein(name, shape, dt=F32):
        ap[name] = nc.dram_tensor(name, list(shape), dt, kind="ExternalInput").ap()
    ein("x", (S, D))
    for w in ["w0q", "w0k", "w0v", "w0o", "w1q", "w1k", "w1v", "w1o"]:
        ein(w, (D, D))
    ein("gw", (D, I0)); ein("uw", (D, I0)); ein("dw", (I0, D))
    ein("fw1", (I1, D)); ein("fw2", (D, I1))
    ein("ident", (128, 128), BF16)
    ein("bias_pp", (128, 3, 16))
    ein("scl_pp", (128, 2, 48))
    ein("bo_bc", (128, D), BF16)
    ein("ds_bc", (128, D), BF16)
    ein("masks", (12, 128, 512), BF16)

    out = nc.dram_tensor("out", [S, D], F32, kind="ExternalOutput").ap()

    dr = {}
    def din(name, shape, dt):
        dr[name] = nc.dram_tensor(name, list(shape), dt, kind="Internal").ap()
    for w in ["w0q", "w0k", "w0v", "w0o", "w1q", "w1k", "w1v", "w1o"]:
        din(w + "_b", (D, D), BF16)
    din("g8", (D, I0), F8); din("u8", (D, I0), F8); din("d8", (I0, D), F8)
    din("t1bf", (I1, D), BF16)
    din("t2bf", (D, I1), BF16)
    din("t1T8", (D, I1), F8)
    din("t2T8", (I1, D), F8)
    din("xq1T", (D, T1), BF16)
    din("xq2T", (I1, 128), BF16)
    din("dbgden", (16, 2048), F32)
    din("dbgE", (24, 128, 512), BF16)
    din("dbgQ", (128, T1), BF16)
    din("dbgH", (10, 128, I1), BF16)
    din("dbgH2", (10, 128, I1), BF16)
    din("dbgST", (2, 128, 64), F32)
    din("dbgK", (128, 1792), BF16)
    din("p1sp", (14 * 128, D), BF16)
    din("ctx1d", (D, T1), BF16)

    sb = {}
    def sbt(name, shape, dt):
        sb[name] = nc.sbuf_tensor(name, list(shape), dt).__enter__()
        return sb[name]

    sbt("XN", [128, 16, 2048], BF16)
    sbt("SL", [128, 28672], BF16)
    sbt("PB", [128, 24576], BF16)
    sbt("PFf", [128, 5632], F32)
    sbt("P8", [128, 12288], F8)
    sbt("IDT", [128, 128], BF16)
    sbt("BIA", [128, 3, 16], F32)
    sbt("SCL", [128, 2, 48], F32)
    sbt("ST", [128, 64], F32)
    sbt("ST2", [128, 64], F32)
    sbt("SMALL", [128, 16], F32)
    sbt("ONECOL", [128, 1], BF16)
    sbt("ONEFC", [128, 1], F32)
    sbt("ONEFR", [1, 128], F32)
    sbt("DEN", [1, 512], BF16)
    sbt("DENR", [1, 512], F32)
    sbt("WACC", [128, 132], F32)

    PS = nc.psum_tensor("PS", [128, 6, 512], F32).__enter__()
    PT = nc.psum_tensor("PT", [128, 8, 128], BF16).__enter__()
    PD = nc.psum_tensor("PD", [128, 512], F32).__enter__()

    sems = {}
    pools = {"y": [], "s": [], "g": []}
    for q, npool in [("y", 8), ("s", 4), ("g", 4)]:
        for i in range(npool):
            nm = f"p{q}{i}"
            sems[nm] = Sem(nc.semaphore(nm).__enter__(), nm)
            pools[q].append(nm)
    for name in ["sCN", "sC0", "sC1", "sC2", "sC3", "sGC", "sPE", "sV", "sS"]:
        sems[name] = Sem(nc.semaphore(name).__enter__(), name)

    pid = nc.partition_id([ET.PE, ET.DVE, ET.Pool, ET.Activation, ET.SP])
    rr = {}
    for eng, nm in [(nc.tensor, "t"), (nc.vector, "v"), (nc.gpsimd, "g"),
                    (nc.scalar, "s"), (nc.sync, "y")]:
        rr[nm] = eng.alloc_register("role_" + nm)

    sCN = sems["sCN"]
    with nc.Block() as b0:
        @b0.sync
        def _(sy):
            sy.reg_alu(rr["y"], pid, 1, ALU.bitwise_and)
            sy.dma_start(out=sb["IDT"][:], in_=ap["ident"]).then_inc(sCN.h, 16)
            sy.dma_start(out=sb["BIA"][:], in_=ap["bias_pp"]).then_inc(sCN.h, 16)
            sy.dma_start(out=sb["SCL"][:], in_=ap["scl_pp"]).then_inc(sCN.h, 16)
        @b0.tensor
        def _(t):
            t.reg_alu(rr["t"], pid, 1, ALU.bitwise_and)
        @b0.vector
        def _(v):
            v.reg_alu(rr["v"], pid, 1, ALU.bitwise_and)
            v.memset(sb["SMALL"][:], 0.0)
            v.memset(sb["WACC"][:], 0.0)
            v.memset(sb["ST"][:, :], 0.0)
            v.memset(sb["ST2"][:, :], 0.0)
            v.memset(sb["DEN"][:, :], 0.0)
            v.memset(sb["DENR"][:, :], 0.0)
        @b0.scalar
        def _(s):
            s.reg_alu(rr["s"], pid, 1, ALU.bitwise_and)
        @b0.gpsimd
        def _(g):
            g.reg_alu(rr["g"], pid, 1, ALU.bitwise_and)
            g.memset(sb["ONECOL"][:], 1.0)
            g.memset(sb["ONEFC"][:], 1.0)
            g.memset(sb["ONEFR"][:], 1.0)
            casts = [("w0k_b", "w0k"), ("w0v_b", "w0v"), ("w0q_b", "w0q"),
                     ("w0o_b", "w0o"), ("g8", "gw"), ("u8", "uw"),
                     ("d8", "dw"), ("w1q_b", "w1q"), ("w1k_b", "w1k"),
                     ("w1v_b", "w1v"), ("w1o_b", "w1o")]
            for i, (name, src) in enumerate(casts):
                cs = sems["sC%d" % (i % 4)]
                if i >= 4:
                    g.wait_ge(cs.h, 16 * (i // 4))
                g.dma_start(out=dr[name], in_=ap[src]).then_inc(cs.h, 16)
    sCN.n += 48
    names = ["w0k_b", "w0v_b", "w0q_b", "w0o_b", "g8", "u8", "d8",
             "w1q_b", "w1k_b", "w1v_b", "w1o_b"]
    cast_done = {}
    for i, nm in enumerate(names):
        cs = "sC%d" % (i % 4)
        cast_done[nm] = (cs, 16 * (i // 4 + 1))
        sems[cs].n = max(sems[cs].n, 16 * (i // 4 + 1))
    consts_done = ("sCN", sCN.n)

    C = dict(nc=nc, ap=ap, dr=dr, sb=sb, out=out, sems=sems, pools=pools,
             PS=PS, PT=PT, PD=PD, cast_done=cast_done, consts_done=consts_done)

    base = {k: v.n for k, v in sems.items()}
    for role in nc.Switch(engines=[nc.tensor, nc.vector, nc.gpsimd, nc.scalar, nc.sync],
                          index=[bass.RuntimeValue(rr[n]) for n in ["t", "v", "g", "s", "y"]],
                          n=2):
        for k, v in base.items():
            sems[k].n = v
        P = Pipe(C)
        emit_role(C, P, Role(role))
        P.emit()
    return nc


def pb(C, off, n):
    return C["sb"]["PB"][:, off:off + n]


def pf(C, off, n):
    return C["sb"]["PFf"][:, off:off + n]


def p8v(C, off, n):
    return C["sb"]["P8"][:, off:off + n]


def emit_role(C, P, R):
    E = {}
    phase_ln0(C, P, R, E)
    if DEBUG_PHASE <= 1:
        dump_sbuf(C, P, R, E, C["sb"]["XN"][:, 0, 0:2048], [E["xn0"]])
        return
    phase_attn0(C, P, R, E)
    if DEBUG_PHASE <= 2:
        dump_sbuf(C, P, R, E, C["sb"]["SL"][:, 0:2048],
                  [E["ctx0"], E["mean_ops"][-1] if E["mean_ops"] else None])
        return
    phase_outproj0(C, P, R, E)
    if DEBUG_PHASE <= 3:
        debug_out_rows(C, P, R, E, E["partial1_ops"], "SLP1")
        return
    phase_ffn0(C, P, R, E)
    if DEBUG_PHASE <= 4:
        dump_sl_rows(C, P, R, E, list(range(R.NT0)), R.l0_lo,
                     [E["tern_done"], E["f0_done"],
                      [E["x1w"][t] for t in E["x1w"]]])
        return
    phase_ln1(C, P, R, E)
    if DEBUG_PHASE <= 5:
        out = C["out"]; dr = C["dr"]; sb = C["sb"]
        bst = pb(C, 0, 2048)
        fst = pf(C, 0, 2048)
        qn = R.q_n
        ld = P.op("y", lambda y: y.dma_start(out=bst[:, 0:qn], in_=dr["xq1T"][0:128, 0:qn]),
                  deps=[[E["xq1_spill"], E["xn1"], E["tern_done"]]], dma=True)
        cp = P.op("v", lambda v: v.tensor_copy(fst[:, 0:qn], bst[:, 0:qn]), deps=[ld])
        s1 = P.op("y", lambda y: y.dma_start(out=out[0:128, 0:qn], in_=fst[:, 0:qn]),
                  deps=[cp], dma=True)
        cp2 = P.op("v", lambda v: v.tensor_copy(fst[:, 0:64], sb["ST2"][:, 0:64]), deps=[s1])
        s2 = P.op("y", lambda y: y.dma_start(out=out[128:256, 0:64], in_=fst[:, 0:64]),
                  deps=[cp2], dma=True)
        P.op("v", lambda v: v.nop(), deps=[s2])
        return
    phase_attn1(C, P, R, E)
    phase_wo1(C, P, R, E)
    if DEBUG_PHASE <= 7 and DEBUG_PHASE != 6:
        qt0 = (R.q_lo - R.l0_lo) // 128
        dump_sl_rows(C, P, R, E, list(range(qt0, R.NT0)), R.l0_lo,
                     [E["wo1_done"], [E["p2w"][q] for q in E["p2w"]]])
        return
    phase_ffn1(C, P, R, E)


def dump_sl_rows(C, P, R, E, tiles, rowbase, deps):
    sb = C["sb"]; out = C["out"]
    SLf = sb["SL"]
    fstg = [pf(C, 0, 2048), pf(C, 2048, 2048)]
    alldone = deps
    cps, sts = [], []
    for n, t_i in enumerate(tiles):
        b = n % 2
        cp = P.op("v", lambda v, t_i=t_i, b=b: v.tensor_copy(
            fstg[b][:], SLf[:, 2048 * t_i:2048 * (t_i + 1)]),
            deps=[alldone, sts[n - 2] if n >= 2 else None])
        cps.append(cp)
        st = P.op("y", lambda y, t_i=t_i, b=b: y.dma_start(
            out=out[rowbase + 128 * t_i:rowbase + 128 * (t_i + 1), :],
            in_=fstg[b][:]), deps=[cp], dma=True)
        sts.append(st)
    P.op("v", lambda v: v.nop(), deps=[sts[-1]])


def dump_sbuf(C, P, R, E, ap_src, deps):
    out = C["out"]
    stg = pf(C, 4096, 1536)
    cp = P.op("v", lambda v: v.tensor_copy(stg[:, 0:1024], ap_src[:, 0:1024]), deps=deps)
    st = P.op("y", lambda y: y.dma_start(out=out[0:128, 0:1024], in_=stg[:, 0:1024]),
              deps=[cp], dma=True)
    P.op("v", lambda v: v.nop(), deps=[st])



def phase_ln0(C, P, R, E):
    sb = C["sb"]; x = C["ap"]["x"]; PT = C["PT"]
    XN, ST, SM, IDT = sb["XN"], sb["ST"], sb["SMALL"], sb["IDT"]
    N = R.NKV
    x0 = [pf(C, 0, 2048), pf(C, 0, 2048)]
    sqf = pf(C, 2048, 2048)
    xntm = [pb(C, 2048, 2048), pb(C, 4096, 2048)]
    MU0, RS0 = 0, 16

    epsop = P.op("v", lambda v: v.memset(SM[:, 14:15], EPS), deps=[C["consts_done"]])
    applies, trs, evacs = [], [], []
    for i in range(N):
        b = i % 2
        ld = P.op("y", lambda y, i=i, b=b: y.dma_start(
            out=x0[b][:], in_=x[R.kv_lo + 128 * i:R.kv_lo + 128 * (i + 1), :]),
            deps=[applies[i - 1] if i >= 1 else None], dma=True)
        sso = P.op("v", lambda v, b=b: v.scalar_tensor_tensor(
            sqf[:], x0[b][:], 1.0, x0[b][:], ALU.mult, ALU.mult,
            accum_out=SM[:, 1:2]), deps=[ld, epsop])
        rso = P.op("v", lambda v, b=b: v.reduce_sum(SM[:, 0:1], x0[b][:], AX.X))
        lack = P.op("s", lambda s: s.nop(), deps=[sso, rso])
        P.op("v", lambda v: v.tensor_scalar(SM[:, 2:3], SM[:, 1:2], 1.0 / D, None, ALU.mult),
             deps=[lack])
        P.op("v", lambda v, i=i: v.tensor_scalar(
            ST[:, MU0 + i:MU0 + i + 1], SM[:, 0:1], 1.0 / D, None, ALU.mult))
        nv = P.op("v", lambda v, i=i: v.scalar_tensor_tensor(
            SM[:, 3:4], ST[:, MU0 + i:MU0 + i + 1], ST[:, MU0 + i:MU0 + i + 1],
            SM[:, 2:3], ALU.mult, ALU.subtract))
        sqr = P.op("s", lambda s: s.activation(
            SM[:, 4:5], SM[:, 3:4], AF.Sqrt, scale=-1.0, bias=SM[:, 14:15]), deps=[nv])
        P.op("v", lambda v, i=i: v.reciprocal(ST[:, RS0 + i:RS0 + i + 1], SM[:, 4:5]),
             deps=[sqr])
        appl = P.op("v", lambda v, i=i, b=b: v.tensor_scalar(
            xntm[b][:], x0[b][:], ST[:, MU0 + i:MU0 + i + 1],
            ST[:, RS0 + i:RS0 + i + 1], ALU.subtract, ALU.mult),
            deps=[trs[4 * (i - 2) + 3] if i >= 2 else None])
        applies.append(appl)
        for g in range(4):
            gi = 4 * i + g
            tr = P.op("t", lambda t, b=b, g=g: [
                t.transpose(PT[:, (g % 2) * 4 + k, :], xntm[b][:, ts(4 * g + k)], IDT[:])
                for k in range(4)][-1],
                deps=[appl, evacs[gi - 1] if gi >= 1 else None])
            trs.append(tr)
            ev = P.op("s", lambda s, i=i, g=g: [
                s.copy(XN[:, 4 * g + k, ts(i)], PT[:, (g % 2) * 4 + k, :])
                for k in range(4)][-1], deps=[tr])
            evacs.append(ev)
    E["xn0"] = evacs[-1]
    E["ln0_x0_free"] = applies[-1]
    E["ln0_last_tr"] = trs[-1]
    E["ln0_mu"] = (MU0, RS0)



class SlotRing:
    def __init__(self):
        self.free = [[] for _ in range(6)]
        self.i = 0

    def take(self, k=1):
        r = []
        for _ in range(k):
            r.append(self.i)
            self.i = (self.i + 1) % 6
        return r


def proj_waves(C, P, ring, wbuf, cc, groups, xn_chunks, deps):
    PS = C["PS"]
    res = []
    for wv in range(0, len(groups), 2):
        wave = groups[wv:wv + 2]
        wslots = ring.take(len(wave))
        d = list(deps)
        for sl in wslots:
            d.extend(ring.free[sl])

        def emit(t, wave=wave, wslots=wslots, wbuf=wbuf, cc=cc):
            last = None
            for dc in range(16):
                t.ldweights(wbuf[:, dc, cc * 128:(cc + 1) * 128])
                for (sl, (off, ln)) in zip(wslots, wave):
                    last = t.matmul(PS[:, sl, 0:ln],
                                    wbuf[:, dc, cc * 128:(cc + 1) * 128],
                                    xn_chunks(dc)[:, off:off + ln],
                                    start=(dc == 0), stop=(dc == 15))
            return last
        op = P.op("t", emit, deps=d)
        for (sl, (off, ln)) in zip(wslots, wave):
            res.append((sl, off, ln, op))
    return res


def phase_attn0(C, P, R, E):
    sb = C["sb"]; dr = C["dr"]; PS = C["PS"]; ap = C["ap"]
    XN, BIA, SM, WACC = sb["XN"], sb["BIA"], sb["SMALL"], sb["WACC"]
    SLf = sb["SL"]
    kvn, l0n, pren = R.kv_n, R.l0_n, R.pre_n
    npairs = 8

    WTb = [sb["PB"][:, 0:12288].rearrange("p (a c n) -> p a c n", a=3, n=256)[:, i]
           for i in range(3)]
    phik = pb(C, 12288, 2048)
    kvs = pb(C, 14336, 2048)
    vb = pb(C, 16384, 2048)
    pkv = pb(C, 18432, 2048)
    ez = pb(C, 20480, 2048)
    onesg = pb(C, 22528, 896)
    wch = pf(C, 0, 2048)

    onesop = P.op("g", lambda g: g.memset(onesg[:], 1.0))

    mean_jobs = [(ap["fw1"], 128 * ch, 0, ch) for ch in range(64)]
    mean_jobs += [(ap["fw2"], 128 * ch, 2048 * p_, 64 + 4 * ch + p_)
                  for ch in range(16) for p_ in range(4)]
    mj = [0]
    mean_ops = []

    def interleave_means(njobs):
        for _ in range(njobs):
            if mj[0] >= len(mean_jobs):
                return
            src, r0, c0, acc = mean_jobs[mj[0]]; mj[0] += 1
            prev = mean_ops[-1] if mean_ops else E.get("ln0_x0_free")
            ld = P.op("s", lambda s, src=src, r0=r0, c0=c0: s.dma_start(
                out=wch[:], in_=src[r0:r0 + 128, c0:c0 + 2048]),
                deps=[prev], dma=True)
            red = P.op("v", lambda v, acc=acc: v.reduce_sum(
                WACC[:, acc:acc + 1], wch[:], AX.X, apply_absolute_value=True),
                deps=[ld])
            mean_ops.append(red)

    ring = SlotRing()
    wsrc = {"k": "w0k_b", "v": "w0v_b", "q": "w0q_b"}
    xbars = {}
    wt_user = {}
    prev_chunk = {}
    prev_chunk = {}

    xnf = lambda dc: XN[:, dc, :]
    ctx_ops = []
    for pr in range(min(npairs, ATTN0_LIMIT)):
        for sub_i, sub in enumerate(("k", "v", "q")):
            bsl = (pr * 3 + sub_i) % 3
            src = dr[wsrc[sub]]
            xb = P.op("y", lambda y, src=src, pr=pr, wbuf=WTb[bsl]: [
                y.dma_start_transpose(wbuf[:, dc, :],
                                      src[256 * pr:256 * (pr + 1), ts(dc)])
                for dc in range(16)],
                deps=[C["cast_done"][wsrc[sub]], wt_user.get(bsl),
                      E["ln0_last_tr"] if pr == 0 else None], dma=True, ndma=16)
            xbars[(pr, sub)] = (xb, bsl)

        for cc in range(2):
            ch = 2 * pr + cc
            xbk, bk = xbars[(pr, "k")]
            ksl = proj_waves(C, P, ring, WTb[bk], cc, R.kv_groups, xnf,
                             [xbk, E["xn0"]])
            wt_user[bk] = ksl[-1][3]
            phi_k_ops = []
            for (sl, off, ln, mm) in ksl:
                ex = P.op("s", lambda s, sl=sl, off=off, ln=ln, ch=ch: s.activation(
                    ez[:, off:off + ln], PS[:, sl, 0:ln], AF.Exp,
                    bias=BIA[:, 1, ch:ch + 1]), deps=[mm, prev_chunk.get("ez")])
                rz = P.op("v", lambda v, sl=sl, off=off, ln=ln, ch=ch: v.tensor_scalar(
                    phik[:, off:off + ln], PS[:, sl, 0:ln], BIA[:, 1, ch:ch + 1], 0.0,
                    ALU.add, ALU.max), deps=[mm, ex, prev_chunk.get("phik")])
                fin = P.op("v", lambda v, off=off, ln=ln: v.scalar_tensor_tensor(
                    phik[:, off:off + ln], ez[:, off:off + ln], 1.0,
                    phik[:, off:off + ln], ALU.min, ALU.add), deps=[ex, rz])
                ring.free[sl] = [rz, ex]
                phi_k_ops.append(fin)
            xbv, bv = xbars[(pr, "v")]
            vsl = proj_waves(C, P, ring, WTb[bv], cc, R.kv_groups, xnf,
                             [xbv, E["xn0"]])
            wt_user[bv] = vsl[-1][3]
            v_ops = []
            for (sl, off, ln, mm) in vsl:
                e_ = P.op("v", lambda v, sl=sl, off=off, ln=ln, ch=ch: v.tensor_scalar(
                    vb[:, off:off + ln], PS[:, sl, 0:ln], BIA[:, 2, ch:ch + 1],
                    None, ALU.add), deps=[mm, prev_chunk.get("vb")])
                ring.free[sl] = [e_]
                v_ops.append(e_)
            pk = P.op("v", lambda v: v.tensor_tensor(
                pkv[:, 0:kvn], phik[:, 0:kvn], vb[:, 0:kvn], ALU.mult),
                deps=phi_k_ops + v_ops + [prev_chunk.get("pkv")])
            if pren:
                ini = P.op("v", lambda v: v.reduce_sum(SM[:, 5:6], pkv[:, 0:pren], AX.X),
                           deps=[pk])
            else:
                ini = P.op("v", lambda v: v.memset(SM[:, 5:6], 0.0))
            ack = P.op("s", lambda s: s.nop(), deps=[ini])
            scans = []
            off = pren
            seg = 0
            while off < kvn:
                ln = min(896, kvn - off)
                sc = P.op("v", lambda v, off=off, ln=ln, seg=seg: v.tensor_tensor_scan(
                    kvs[:, off - pren:off - pren + ln], onesg[:, 0:ln],
                    pkv[:, off:off + ln],
                    SM[:, 5:6] if seg == 0 else kvs[:, off - pren - 1:off - pren],
                    ALU.mult, ALU.add),
                    deps=[pk, ini, onesop, prev_chunk.get("kvs"), ack])
                if off + ln < kvn:
                    ack = P.op("s", lambda s: s.nop(), deps=[sc])
                scans.append(sc)
                off += ln
                seg += 1
            xbq, bq = xbars[(pr, "q")]
            qsl = proj_waves(C, P, ring, WTb[bq], cc, R.l0_groups, xnf,
                             [xbq, E["xn0"]])
            wt_user[bq] = qsl[-1][3]
            base_off = R.l0_groups[0][0]
            phi_q_ops = []
            for (sl, off_, ln, mm) in qsl:
                o = off_ - base_off
                ex = P.op("s", lambda s, sl=sl, o=o, ln=ln, ch=ch: s.activation(
                    ez[:, o:o + ln], PS[:, sl, 0:ln], AF.Exp,
                    bias=BIA[:, 0, ch:ch + 1]),
                    deps=[mm, prev_chunk.get("ez")] + phi_k_ops)
                rz = P.op("v", lambda v, sl=sl, o=o, ln=ln, ch=ch: v.tensor_scalar(
                    pkv[:, o:o + ln], PS[:, sl, 0:ln], BIA[:, 0, ch:ch + 1], 0.0,
                    ALU.add, ALU.max), deps=[mm, ex] + scans)
                fin = P.op("v", lambda v, o=o, ln=ln: v.scalar_tensor_tensor(
                    pkv[:, o:o + ln], ez[:, o:o + ln], 1.0,
                    pkv[:, o:o + ln], ALU.min, ALU.add), deps=[ex, rz])
                ring.free[sl] = [rz, ex]
                phi_q_ops.append(fin)
            cx = P.op("v", lambda v, ch=ch: v.scalar_tensor_tensor(
                SLf[:, ch * l0n:(ch + 1) * l0n], pkv[:, 0:l0n], SCALE0,
                kvs[:, 0:l0n], ALU.mult, ALU.mult), deps=phi_q_ops + scans)
            ctx_ops.append(cx)
            prev_chunk = {"phik": pk, "vb": pk, "pkv": cx, "ez": phi_q_ops[-1],
                          "kvs": cx}
        interleave_means(16)
    interleave_means(len(mean_jobs))
    E["ctx0"] = ctx_ops
    E["ring"] = ring
    E["mean_ops"] = mean_ops



def phase_outproj0(C, P, R, E):
    sb = C["sb"]; dr = C["dr"]; PS = C["PS"]; ap = C["ap"]
    SLf = sb["SL"]
    l0n = R.l0_n
    ring = E["ring"]
    WoT = [sb["PB"][:, 0:16384].rearrange("p (a c n) -> p a c n", a=2, n=512)[:, i]
           for i in range(2)]
    bo = pb(C, 16384, 2048)
    stg = [pb(C, 18432, 512), pb(C, 18944, 512)]
    xq = [pf(C, 4096, 512), pf(C, 4608, 512)]
    x = ap["x"]

    attn_done = E["ctx0"][-1]
    boop = P.op("y", lambda y: y.dma_start(out=bo[:], in_=ap["bo_bc"]),
                deps=[attn_done], dma=True)
    ctx_chunk = lambda c: SLf[:, c * l0n:(c + 1) * l0n]

    last_mm_of_eq = {}
    p1_stores = []
    ev_prev = {}
    for eq in range(4):
        xb = P.op("y", lambda y, eq=eq, buf=WoT[eq % 2]: [
            y.dma_start_transpose(buf[:, dc, :],
                                  dr["w0o_b"][512 * eq:512 * (eq + 1), ts(dc)])
            for dc in range(16)],
            deps=[C["cast_done"]["w0o_b"], attn_done, last_mm_of_eq.get(eq - 1)],
            dma=True, ndma=16)
        for t_i in range(R.NT0):
            b = t_i % 2
            xl = P.op("y", lambda y, t_i=t_i, eq=eq, b=b: y.dma_start(
                out=xq[b][:],
                in_=x[R.l0_lo + 128 * t_i:R.l0_lo + 128 * (t_i + 1),
                      512 * eq:512 * (eq + 1)]),
                deps=[ev_prev.get((eq, t_i - 2))], dma=True)
            (sl,) = ring.take(1)
            mm = P.op("t", lambda t, eq=eq, t_i=t_i, sl=sl, buf=WoT[eq % 2]: [
                t.matmul(PS[:, sl, :], ctx_chunk(c)[:, ts(t_i)], buf[:, c, :],
                         start=(c == 0), stop=(c == 15)) for c in range(16)][-1],
                deps=[xb] + ring.free[sl])
            last_mm_of_eq[eq] = mm
            e1 = P.op("v", lambda v, sl=sl, b=b: v.tensor_tensor(
                stg[b][:], PS[:, sl, :], xq[b][:], ALU.add),
                deps=[mm, xl, p1_stores[-2] if len(p1_stores) >= 2 else None])
            e2 = P.op("v", lambda v, eq=eq, b=b: v.tensor_tensor(
                stg[b][:], stg[b][:], bo[:, ts(eq, 512)], ALU.add),
                deps=[e1, boop])
            ring.free[sl] = [e2]
            ev_prev[(eq, t_i)] = e1
            st = P.op("y", lambda y, t_i=t_i, eq=eq, b=b: y.dma_start(
                out=dr["p1sp"][128 * t_i:128 * (t_i + 1), 512 * eq:512 * (eq + 1)],
                in_=stg[b][:]), deps=[e2], dma=True)
            p1_stores.append(st)
    E["partial1_ops"] = [p1_stores[-1]] * R.NT0
    E["p1_all_done"] = p1_stores[-1]
    E["op0_done"] = [last_mm_of_eq[3], p1_stores[-1]]



def phase_ffn0(C, P, R, E):
    sb = C["sb"]; dr = C["dr"]; PS = C["PS"]; ap = C["ap"]
    SLf, XN, SCL, SM, WACC = sb["SL"], sb["XN"], sb["SCL"], sb["SMALL"], sb["WACC"]
    ring = E["ring"]
    l0o = R.l0_lo - R.kv_lo

    up_s = sb["PB"][:, 0:18432].rearrange("p (c n) -> p c n", n=384)
    p1r = pb(C, 18432, 2048)
    dsb = pb(C, 20480, 2048)
    ternb = pb(C, 22528, 2048)
    band8 = [sb["P8"][:, 0:6144].rearrange("p (c n) -> p c n", n=384),
             sb["P8"][:, 6144:12288].rearrange("p (c n) -> p c n", n=384)]
    dstr = [p8v(C, i * 1024, 1024) for i in range(4)]
    tmp3 = [pf(C, 0, 1152).rearrange("p (c n) -> p c n", n=384),
            pf(C, 1152, 1152).rearrange("p (c n) -> p c n", n=384)]
    wch = pf(C, 2304, 2048)
    dstg = pf(C, 4352, 512)

    op0 = E["op0_done"]
    dsop = P.op("y", lambda y: y.dma_start(out=dsb[:], in_=ap["ds_bc"]),
                deps=[op0], dma=True)

    mprev = E["mean_ops"][-1]
    r1_ = P.op("v", lambda v: v.reduce_sum(SM[:, 6:7], WACC[:, 0:64], AX.X), deps=[mprev])
    r2_ = P.op("v", lambda v: v.reduce_sum(SM[:, 7:8], WACC[:, 64:128], AX.X))
    OFC, OFR = sb["ONEFC"], sb["ONEFR"]
    PD = C["PD"]
    red1 = P.op("t", lambda t: t.matmul(PD[0:1, 0:1], OFC[:], SM[:, 6:7],
                                        start=True, stop=True), deps=[r1_])
    e1_ = P.op("v", lambda v: v.tensor_copy(SM[0:1, 6:7], PD[0:1, 0:1]), deps=[red1])
    red2 = P.op("t", lambda t: t.matmul(PD[0:1, 1:2], OFC[:], SM[:, 7:8],
                                        start=True, stop=True), deps=[r2_, e1_])
    e2_ = P.op("v", lambda v: v.tensor_copy(SM[0:1, 7:8], PD[0:1, 1:2]), deps=[red2])
    bc1 = P.op("t", lambda t: t.matmul(PD[:, 2:3], OFR[:], SM[0:1, 6:7],
                                       start=True, stop=True), deps=[e1_, e2_])
    b1 = P.op("v", lambda v: v.tensor_copy(SM[:, 8:9], PD[:, 2:3]), deps=[bc1])
    bc2 = P.op("t", lambda t: t.matmul(PD[:, 3:4], OFR[:], SM[0:1, 7:8],
                                       start=True, stop=True), deps=[e2_, b1])
    b2 = P.op("v", lambda v: v.tensor_copy(SM[:, 9:10], PD[:, 3:4]), deps=[bc2])
    mack = P.op("s", lambda s: s.nop(), deps=[b1, b2])
    m1 = P.op("v", lambda v: v.tensor_scalar(SM[:, 8:9], SM[:, 8:9], 1.0 / 16777216.0,
                                             EPS, ALU.mult, ALU.max), deps=[b1, mack])
    m2 = P.op("v", lambda v: v.tensor_scalar(SM[:, 9:10], SM[:, 9:10], 1.0 / 16777216.0,
                                             EPS, ALU.mult, ALU.max), deps=[b2, mack])
    mack2 = P.op("s", lambda s: s.nop(), deps=[m1, m2])
    sw1 = P.op("v", lambda v: v.reciprocal(SM[:, 10:11], SM[:, 8:9]), deps=[m1, mack2])
    sw2 = P.op("v", lambda v: v.reciprocal(SM[:, 11:12], SM[:, 9:10]), deps=[m2, mack2])
    E["m1w"], E["m2w"] = m1, m2

    qjobs = [("fw1", "t1bf", 128 * ch, 0, 0) for ch in range(64)]
    qjobs += [("fw2", "t2bf", 128 * ch, 2048 * p_, 1) for ch in range(16) for p_ in range(4)]
    qi = [0]
    qprev = {"v1": None, "st": None}
    tern_stores = []

    def interleave_quant(n):
        for _ in range(n):
            if qi[0] >= len(qjobs):
                return
            srcn, dstn, r0, c0, wi = qjobs[qi[0]]; qi[0] += 1
            swop = sw1 if wi == 0 else sw2
            ld = P.op("s", lambda s, srcn=srcn, r0=r0, c0=c0: s.dma_start(
                out=wch[:], in_=ap[srcn][r0:r0 + 128, c0:c0 + 2048]),
                deps=[qprev["v1"], swop], dma=True)
            v1 = P.op("v", lambda v, wi=wi: v.tensor_scalar(
                wch[:], wch[:], SM[:, 10 + wi:11 + wi], MAGIC, ALU.mult, ALU.add),
                deps=[ld])
            v2 = P.op("v", lambda v: v.tensor_scalar(
                ternb[:], wch[:], MAGIC, 1.0, ALU.subtract, ALU.min),
                deps=[qprev["st"]])
            v3 = P.op("v", lambda v: v.tensor_scalar(
                ternb[:], ternb[:], -1.0, None, ALU.max))
            st = P.op("y", lambda y, dstn=dstn, r0=r0, c0=c0: y.dma_start(
                out=dr[dstn][r0:r0 + 128, c0:c0 + 2048], in_=ternb[:]),
                deps=[v3], dma=True)
            qprev["v1"] = v1
            qprev["st"] = st
            tern_stores.append(st)

    x1w = {t: [] for t in range(R.NT0)}
    prev_down_last = None
    prev_up_reader = None
    prev_p1 = {}
    for ci, (coff, cn) in enumerate(R.f0_chunks):
        fuse_ops = []
        up_ev = {}
        gate_last_mm = None
        up_band_mm = None
        for wi, (w8, scl_i) in enumerate([("u8", 1), ("g8", 0)]):
            band_mm = []
            prev_sil = {}
            for bd in range(16):
                bb = band8[bd % 2]
                cross = None
                if bd < 2:
                    cross = prev_down_last if wi == 0 else up_band_mm[14 + bd]
                ld = P.op("y", lambda y, w8=w8, bd=bd, bb=bb: y.dma_start(
                    out=bb[:], in_=dr[w8].rearrange("(c p) o -> p c o", p=128)
                    [:, :, 384 * bd:384 * (bd + 1)]),
                    deps=[C["cast_done"][w8], op0, cross,
                          band_mm[bd - 2] if bd >= 2 else None], dma=True)
                slots = ring.take(3)
                d = [ld, E["xn0"], prev_up_reader]
                for sl in slots:
                    d.extend(ring.free[sl])

                def mmf(t, bb=bb, slots=tuple(slots), coff=coff, cn=cn):
                    last = None
                    for dc in range(16):
                        for j, sl in enumerate(slots):
                            t.ldweights(bb[:, dc, 128 * j:128 * (j + 1)])
                            last = t.matmul(
                                PS[:, sl, 0:cn], bb[:, dc, 128 * j:128 * (j + 1)],
                                XN[:, dc, l0o + coff:l0o + coff + cn],
                                start=(dc == 0), stop=(dc == 15))
                    return last
                mm = P.op("t", mmf, deps=d)
                band_mm.append(mm)
                for j, sl in enumerate(slots):
                    oc = 3 * bd + j
                    if wi == 0:
                        ev = P.op("v", lambda v, sl=sl, oc=oc, cn=cn: v.tensor_scalar(
                            up_s[:, oc, 0:cn], PS[:, sl, 0:cn],
                            SCL[:, 1, oc:oc + 1], None, ALU.mult),
                            deps=[mm, prev_up_reader])
                        ring.free[sl] = [ev]
                        up_ev[oc] = ev
                    else:
                        tb = tmp3[bd % 2]
                        sil = P.op("s", lambda s, sl=sl, j=j, oc=oc, cn=cn, tb=tb: s.activation(
                            tb[:, j, 0:cn], PS[:, sl, 0:cn], AF.Silu,
                            scale=SCL[:, 0, oc:oc + 1]),
                            deps=[mm, prev_sil.get(bd - 2)])
                        fu = P.op("v", lambda v, j=j, oc=oc, cn=cn, tb=tb: v.tensor_tensor(
                            up_s[:, oc, 0:cn], tb[:, j, 0:cn], up_s[:, oc, 0:cn],
                            ALU.mult), deps=[sil, up_ev[oc]])
                        ring.free[sl] = [sil]
                        fuse_ops.append(fu)
                        prev_sil[bd] = fu
                        gate_last_mm = mm
            if wi == 0:
                up_band_mm = band_mm
        tiles = [coff // 128 + k for k in range(cn // 128)]
        if "prev_dstr_mms" not in E:
            E["prev_dstr_mms"] = None
        for pi in range(0, len(tiles), 2):
            pair = tiles[pi:pi + 2]
            for half in range(2):
                slots = {}
                sdeps = []
                for t_i in pair:
                    ss = ring.take(2)
                    slots[t_i] = ss
                    for sl in ss:
                        sdeps.extend(ring.free[sl])
                mmops = []
                for i2c in range(48):
                    rb = dstr[i2c % 4]
                    pdm = E["prev_dstr_mms"]
                    ld = P.op("y", lambda y, i2c=i2c, half=half, rb=rb: y.dma_start(
                        out=rb[:], in_=dr["d8"][128 * i2c:128 * (i2c + 1),
                                                1024 * half:1024 * (half + 1)]),
                        deps=[C["cast_done"]["d8"],
                              gate_last_mm if i2c < 4 else None,
                              (pdm[44 + i2c] if pdm is not None else None) if i2c < 4 else None,
                              mmops[i2c - 4] if i2c >= 4 else None], dma=True)

                    def dmmf(t, i2c=i2c, pair=tuple(pair),
                             slots={k: tuple(v) for k, v in slots.items()},
                             coff=coff, rb=rb):
                        last = None
                        for t_i in pair:
                            tloc = 128 * t_i - coff
                            t.ldweights(up_s[:, i2c, tloc:tloc + 128])
                            for j, sl in enumerate(slots[t_i]):
                                last = t.matmul(
                                    PS[:, sl, :], up_s[:, i2c, tloc:tloc + 128],
                                    rb[:, 512 * j:512 * (j + 1)],
                                    start=(i2c == 0), stop=(i2c == 47))
                        return last
                    mm = P.op("t", dmmf,
                              deps=[ld] + (fuse_ops + sdeps if i2c == 0 else []))
                    mmops.append(mm)
                prev_down_last = mmops[-1]
                E["prev_dstr_mms"] = mmops
                for t_i in pair:
                    p1l = P.op("y", lambda y, t_i=t_i, half=half: y.dma_start(
                        out=p1r[:, 1024 * half:1024 * (half + 1)],
                        in_=dr["p1sp"][128 * t_i:128 * (t_i + 1),
                                       1024 * half:1024 * (half + 1)]),
                        deps=[E["p1_all_done"], prev_p1.get(half), dsop],
                        dma=True)
                    for j, sl in enumerate(slots[t_i]):
                        cols = 1024 * half + 512 * j
                        e1 = P.op("v", lambda v, sl=sl, cols=cols: v.tensor_tensor(
                            dstg[:], PS[:, sl, :], dsb[:, cols:cols + 512],
                            ALU.mult), deps=[mmops[-1], dsop])
                        e2 = P.op("v", lambda v, cols=cols, t_i=t_i: v.tensor_tensor(
                            SLf[:, 2048 * t_i + cols:2048 * t_i + cols + 512],
                            dstg[:], p1r[:, cols:cols + 512], ALU.add),
                            deps=[e1, p1l, op0])
                        ring.free[sl] = [e1]
                        x1w[t_i].append(e2)
                        prev_p1[half] = e2
        prev_up_reader = prev_down_last
        interleave_quant(32)
    interleave_quant(len(qjobs))
    E["x1w"] = x1w
    E["tern_done"] = tern_stores[-1] if tern_stores else None
    E["f0_done"] = prev_down_last
    E["xn0_free"] = prev_down_last



def phase_ln1(C, P, R, E):
    sb = C["sb"]; dr = C["dr"]; PT = C["PT"]
    SLf, XN, ST, ST2, SM, IDT = (sb["SL"], sb["XN"], sb["ST"], sb["ST2"],
                                 sb["SMALL"], sb["IDT"])
    MU1, RS1 = 32, 48
    R1c, RDENc, S2Tc, C1c = 0, 16, 32, 48
    N = R.NT0
    qt_first = (R.q_lo - R.l0_lo) // 128

    xntm = [pb(C, 0, 2048), pb(C, 2048, 2048)]
    xqtm = pb(C, 4096, 2048)
    stg = [pb(C, 6144, 2048), pb(C, 8192, 2048)]
    sqf = pf(C, 0, 2048)
    qf = pf(C, 2048, 2048)

    applies, trs, evacs = [], [], []
    spills = []
    xn_free = E["xn0_free"]
    for i in range(N):
        b = i % 2
        x1t = SLf[:, 2048 * i:2048 * (i + 1)]
        st0 = P.op("v", lambda v, x1t=x1t: v.scalar_tensor_tensor(
            sqf[:], x1t, 1.0, x1t, ALU.mult, ALU.mult, accum_out=SM[:, 1:2]),
            deps=[E["x1w"][i]])
        rso1 = P.op("v", lambda v, x1t=x1t: v.reduce_sum(SM[:, 0:1], x1t, AX.X))
        lack1 = P.op("s", lambda s: s.nop(), deps=[st0, rso1])
        P.op("v", lambda v: v.tensor_scalar(SM[:, 2:3], SM[:, 1:2], 1.0 / D, None, ALU.mult),
             deps=[lack1])
        P.op("v", lambda v, i=i: v.tensor_scalar(
            ST[:, MU1 + i:MU1 + i + 1], SM[:, 0:1], 1.0 / D, None, ALU.mult))
        nv = P.op("v", lambda v, i=i: v.scalar_tensor_tensor(
            SM[:, 3:4], ST[:, MU1 + i:MU1 + i + 1], ST[:, MU1 + i:MU1 + i + 1],
            SM[:, 2:3], ALU.mult, ALU.subtract))
        sqr = P.op("s", lambda s: s.activation(
            SM[:, 4:5], SM[:, 3:4], AF.Sqrt, scale=-1.0, bias=SM[:, 14:15]), deps=[nv])
        P.op("v", lambda v, i=i: v.reciprocal(ST[:, RS1 + i:RS1 + i + 1], SM[:, 4:5]),
             deps=[sqr])
        appl = P.op("v", lambda v, i=i, b=b, x1t=x1t: v.tensor_scalar(
            xntm[b][:], x1t, ST[:, MU1 + i:MU1 + i + 1],
            ST[:, RS1 + i:RS1 + i + 1], ALU.subtract, ALU.mult),
            deps=[trs[4 * (i - 2) + 3] if i >= 2 else None,
                  E["f0_done"]])
        applies.append(appl)
        for g in range(4):
            gi = 4 * i + g
            tr = P.op("t", lambda t, b=b, g=g: [
                t.transpose(PT[:, (g % 2) * 4 + k, :], xntm[b][:, ts(4 * g + k)], IDT[:])
                for k in range(4)][-1],
                deps=[appl, xn_free, evacs[gi - 1] if gi >= 1 else None,
                      E.get("_ln1_ev2")])
            trs.append(tr)
            ev = P.op("s", lambda s, i=i, g=g: [
                s.copy(XN[:, 4 * g + k, ts(i)], PT[:, (g % 2) * 4 + k, :])
                for k in range(4)][-1], deps=[tr])
            evacs.append(ev)
        if i >= qt_first:
            qt = i - qt_first
            am = P.op("v", lambda v, b=b: v.reduce_max(
                SM[:, 5:6], xntm[b][:], AX.X, apply_absolute_value=True), deps=[appl])
            nsq = P.op("v", lambda v, b=b: v.scalar_tensor_tensor(
                sqf[:], xntm[b][:], 1.0, xntm[b][:], ALU.mult, ALU.mult,
                accum_out=SM[:, 6:7]))
            nack = P.op("s", lambda s: s.nop(), deps=[nsq, am])
            P.op("v", lambda v: v.tensor_scalar(SM[:, 6:7], SM[:, 6:7], 1e-24, None,
                                                ALU.max), deps=[nack])
            rq = P.op("v", lambda v: v.reciprocal(SM[:, 7:8], SM[:, 6:7]))
            sq2 = P.op("s", lambda s: s.activation(SM[:, 7:8], SM[:, 7:8], AF.Sqrt),
                       deps=[rq])
            P.op("v", lambda v, qt=qt: v.scalar_tensor_tensor(
                ST2[:, R1c + qt:R1c + qt + 1], SM[:, 5:6],
                float(np.sqrt(D) / 127.0), SM[:, 7:8], ALU.mult, ALU.mult),
                deps=[sq2])
            tc1 = P.op("v", lambda v, qt=qt: v.tensor_scalar(
                ST2[:, C1c + qt:C1c + qt + 1], SM[:, 5:6], 1.0 / 127.0, None,
                ALU.mult), deps=[sq2])
            ak1 = P.op("s", lambda s: s.nop(), deps=[tc1])
            c1 = P.op("v", lambda v, qt=qt: v.reciprocal(
                ST2[:, C1c + qt:C1c + qt + 1], ST2[:, C1c + qt:C1c + qt + 1]),
                deps=[ak1])
            akq = P.op("s", lambda s: s.nop(), deps=[c1])
            q1 = P.op("v", lambda v, b=b, qt=qt: v.tensor_scalar(
                qf[:], xntm[b][:], ST2[:, C1c + qt:C1c + qt + 1], MAGIC,
                ALU.mult, ALU.add), deps=[akq])
            q2 = P.op("v", lambda v: v.tensor_scalar(
                xqtm[:], qf[:], MAGIC, 127.0, ALU.subtract, ALU.min),
                deps=[trs[-1] if False else None])
            q3 = P.op("v", lambda v: v.tensor_scalar(
                xqtm[:], xqtm[:], -128.0, None, ALU.max))
            sb_ = qt % 2
            tq = P.op("t", lambda t, sb_=sb_: [
                t.transpose(PT[:, k % 8, :], xqtm[:, ts(k)], IDT[:])
                for k in range(8)][-1],
                deps=[q3, evacs[-1], spills[qt - 1] if qt >= 1 else None])
            ev1 = P.op("v", lambda v, sb_=sb_: [
                v.tensor_copy(stg[sb_][:, ts(k)], PT[:, k % 8, :]) for k in range(8)][-1],
                deps=[tq])
            tq2 = P.op("t", lambda t, sb_=sb_: [
                t.transpose(PT[:, k % 8, :], xqtm[:, ts(8 + k)], IDT[:])
                for k in range(8)][-1], deps=[ev1])
            ev2 = P.op("v", lambda v, sb_=sb_: [
                v.tensor_copy(stg[sb_][:, ts(8 + k)], PT[:, k % 8, :]) for k in range(8)][-1],
                deps=[tq2])
            E["_ln1_ev2"] = ev2
            sp = P.op("y", lambda y, qt=qt, sb_=sb_: y.dma_start(
                out=dr["xq1T"].rearrange("(c p) t -> p c t", p=128)[:, :, ts(qt)],
                in_=stg[sb_].rearrange("p (c n) -> p c n", n=128)[:]),
                deps=[ev2], dma=True)
            spills.append(sp)
    E["xn1"] = evacs[-1]
    E["ln1_trs_last"] = trs[-1]
    E["xq1_spill"] = spills[-1]
    E["ln1_stats"] = (MU1, RS1, R1c, RDENc, S2Tc, C1c)
    E["ln1_pb_free"] = [applies[-1], evacs[-1], spills[-1]]



def phase_attn1(C, P, R, E):
    sb = C["sb"]; dr = C["dr"]; PS = C["PS"]; PD = C["PD"]; ap = C["ap"]
    XN, SM, ST2, ONEC, DEN = sb["XN"], sb["SMALL"], sb["ST2"], sb["ONECOL"], sb["DEN"]
    RDENc = 16
    l0n = R.l0_n
    qoff = R.q_lo - R.l0_lo

    o0 = 8192
    o1 = o0 + 2 * l0n
    o2_ = o1 + R.NTK * 256
    o3 = o2_ + 2 * R.q_n
    o4 = o3 + 2048
    o5 = o4 + 4096
    assert o5 + 1024 <= 24576, (o5,)
    wt8 = [sb["PB"][:, 0:8192].rearrange("p (a c n) -> p a c n", a=2, n=256)[:, i]
           for i in range(2)]
    k8 = sb["PB"][:, o0:o1].rearrange("p (a n) -> p a n", a=2)
    v8 = sb["PB"][:, o1:o2_].rearrange("p (a n) -> p a n", n=256)
    q8 = sb["PB"][:, o2_:o3].rearrange("p (a n) -> p a n", a=2)
    EB = sb["PB"][:, o3:o4].rearrange("p (a n) -> p a n", n=512)
    MB = sb["PB"][:, o4:o5].rearrange("p (a n) -> p a n", n=512)
    cst = [pb(C, o5, 512), pb(C, o5 + 512, 512)]
    t1f = [pf(C, 0, 1024), pf(C, 1024, 1024)]

    pbfree = E["ln1_pb_free"]
    OFR = sb["ONEFR"]
    DENR = sb["DENR"]
    pj_count = [0]
    combo_rows = {r0: j for j, r0 in enumerate([-3, -2, -1, 0, 5, 6, 7, 8])}
    mload = P.op("y", lambda y: [
        y.dma_start(out=MB[:, j, :], in_=ap["masks"][r0 + 3])
        for r0, j in combo_rows.items()],
        deps=[pbfree, E["tern_done"]], dma=True, ndma=8)

    ring = E["ring"]
    wsrcs = {"q": "w1q_b", "k": "w1k_b", "v": "w1v_b"}
    ctx_spills = []
    prev8 = {}
    wtbi = [0]
    wt_user = {}
    for e8 in range(8):
        def make_xbar(W, e8=e8):
            bi = wtbi[0] % 2; wtbi[0] += 1
            xb = P.op("y", lambda y, W=W, e8=e8, bi=bi: [
                y.dma_start_transpose(wt8[bi][:, dc, :],
                                      dr[wsrcs[W]][256 * e8:256 * (e8 + 1), ts(dc)])
                for dc in range(16)],
                deps=[C["cast_done"][wsrcs[W]], wt_user.get(bi), pbfree,
                      E["ln1_trs_last"]],
                dma=True, ndma=16)
            return xb, bi
        qk_ev = {}
        for W in ("q", "k"):
            xb, bi = make_xbar(W)
            groups = ([(qoff + o, ln) for (o, ln) in R.qgroups] if W == "q"
                      else [(o, min(512, l0n - o)) for o in range(0, l0n, 512)])
            dst = q8 if W == "q" else k8
            for hl in range(2):
                for gj, (off, ln) in enumerate(groups):
                    (sl,) = ring.take(1)
                    d = [xb, E["xn1"],
                         prev8.get(("qfree", hl) if W == "q" else ("kfree", hl))] \
                        + ring.free[sl]

                    def pmf(t, bi=bi, hl=hl, off=off, ln=ln, sl=sl):
                        last = None
                        for dc in range(16):
                            t.ldweights(wt8[bi][:, dc, 128 * hl:128 * (hl + 1)])
                            last = t.matmul(PS[:, sl, 0:ln],
                                            wt8[bi][:, dc, 128 * hl:128 * (hl + 1)],
                                            XN[:, dc, off:off + ln],
                                            start=(dc == 0), stop=(dc == 15))
                        return last
                    mm = P.op("t", pmf, deps=d)
                    wt_user[bi] = mm
                    o2 = off - qoff if W == "q" else off
                    ev = P.op("v", lambda v, sl=sl, hl=hl, o2=o2, ln=ln, dst=dst: v.tensor_copy(
                        dst[:, hl, o2:o2 + ln], PS[:, sl, 0:ln]), deps=[mm])
                    ring.free[sl] = [ev]
                    qk_ev[(W, hl)] = ev
        xb, bi = make_xbar("v")
        v_ev = None
        for t_i in range(R.NTK):
            (sl,) = ring.take(1)
            d = [xb, E["xn1"], prev8.get(("vfree",))] + ring.free[sl]

            def vmf(t, bi=bi, t_i=t_i, sl=sl):
                last = None
                for dc in range(16):
                    t.ldweights(XN[:, dc, ts(t_i)])
                    last = t.matmul(PS[:, sl, 0:256], XN[:, dc, ts(t_i)],
                                    wt8[bi][:, dc, :],
                                    start=(dc == 0), stop=(dc == 15))
                return last
            mm = P.op("t", vmf, deps=d)
            wt_user[bi] = mm
            ev = P.op("v", lambda v, sl=sl, t_i=t_i: v.tensor_copy(
                v8[:, t_i, :], PS[:, sl, 0:256]), deps=[mm])
            ring.free[sl] = [ev]
            v_ev = ev
        E["xn1_free"] = wt_user[bi]
        for hl in range(2):
            for gj, (off, ln) in enumerate(R.qgroups):
                qt0g = (R.q_lo + off) // 128
                nb = ln // 128
                jtmin = max(R.kt0, qt0g - 8)
                jtmax = qt0g + nb - 1
                jts = list(range(jtmin, jtmax + 1))
                cxsl = 4 + (E.setdefault("_pvpar", [0])[0] % 2)
                E["_pvpar"][0] += 1
                cxdeps = list(ring.free[cxsl])
                pv_mms = []
                den_mms = []
                ebi = [0]
                for pj in range(0, len(jts), 2):
                    jtp = jts[pj:pj + 2]
                    ssl = [(pj % 4) + k for k in range(len(jtp))]
                    d = [qk_ev[("q", hl)], qk_ev[("k", hl)]]
                    for sl in ssl:
                        d.extend(ring.free[sl])

                    def smf(t, jtp=tuple(jtp), ssl=tuple(ssl), hl=hl, off=off, ln=ln):
                        last = None
                        for jt, sl in zip(jtp, ssl):
                            jl = jt - R.kt0
                            t.ldweights(k8[:, hl, 128 * jl:128 * (jl + 1)])
                            last = t.matmul(PS[:, sl, 0:ln],
                                            k8[:, hl, 128 * jl:128 * (jl + 1)],
                                            q8[:, hl, off:off + ln],
                                            start=True, stop=True)
                        return last
                    smm = P.op("t", smf, deps=d)
                    tb = t1f[(pj // 2) % 2]
                    for jt, sl in zip(jtp, ssl):
                        eb = ebi[0] % 4; ebi[0] += 1
                        th1 = P.op("s", lambda s, sl=sl, ln=ln, tb=tb, jt=jt: s.activation(
                            tb[:, 0:ln] if jt == jtp[0] else tb[:, 512:512 + ln],
                            PS[:, sl, 0:ln], AF.Tanh, scale=TANH_SC), deps=[smm])
                        ex = P.op("s", lambda s, eb=eb, ln=ln, tb=tb, jt=jt: s.activation(
                            EB[:, eb, 0:ln],
                            tb[:, 0:ln] if jt == jtp[0] else tb[:, 512:512 + ln],
                            AF.Exp, scale=30.0),
                            deps=[th1, prev8.get(("eb", eb))])
                        ring.free[sl] = [th1]
                        r0 = qt0g - jt
                        needm = not all(1 <= (r0 + b) <= 7 for b in range(nb))
                        if needm:
                            mops = [ex, mload]
                            mk = P.op("v", lambda v, eb=eb, ln=ln, r0=r0: v.tensor_tensor(
                                EB[:, eb, 0:ln], EB[:, eb, 0:ln],
                                MB[:, combo_rows[r0], 0:ln], ALU.mult), deps=mops)
                            edone = mk
                        else:
                            edone = ex
                        dmm = P.op("t", lambda t, eb=eb, ln=ln, first=(jt == jts[0]),
                                   last=(jt == jts[-1]): t.matmul(
                            PD[0:1, 0:ln], ONEC[:], EB[:, eb, 0:ln],
                            start=first, stop=last),
                            deps=[edone, prev8.get(("denfree",))])
                        den_mms.append(dmm)
                        jl = jt - R.kt0
                        pv = P.op("t", lambda t, eb=eb, ln=ln, jl=jl, hl=hl, cxsl=cxsl,
                                  first=(jt == jts[0]), last=(jt == jts[-1]): (
                            t.ldweights(v8[:, jl, 128 * hl:128 * (hl + 1)]),
                            t.matmul(PS[:, cxsl, 0:ln], v8[:, jl, 128 * hl:128 * (hl + 1)],
                                     EB[:, eb, 0:ln], start=first, stop=last))[-1],
                            deps=[edone, v_ev] + (cxdeps if jt == jts[0] else []))
                        pv_mms.append(pv)
                        prev8[("eb", eb)] = pv
                dev = P.op("v", lambda v, ln=ln: v.reciprocal(DENR[0:1, 0:ln], PD[0:1, 0:ln]),
                           deps=[den_mms[-1], prev8.get(("denfree",))])
                bsl = (pj_count[0] % 4)
                pj_count[0] += 1
                bcm = P.op("t", lambda t, ln=ln, bsl=bsl: t.matmul(
                    PS[:, bsl, 0:ln], OFR[:], DENR[0:1, 0:ln], start=True, stop=True),
                    deps=[dev] + ring.free[bsl])
                prev8[("denfree",)] = bcm
                head_glob = e8 * 2 + hl
                cb = (head_glob + gj) % 2
                cev = P.op("v", lambda v, cxsl=cxsl, ln=ln, cb=cb: v.tensor_copy(
                    cst[cb][:, 0:ln], PS[:, cxsl, 0:ln]),
                    deps=[pv_mms[-1], prev8.get(("cst", cb)), mload])
                ring.free[cxsl] = [cev]
                dv = P.op("v", lambda v, ln=ln, cb=cb, bsl=bsl: v.tensor_tensor(
                    cst[cb][:, 0:ln], cst[cb][:, 0:ln], PS[:, bsl, 0:ln], ALU.mult),
                    deps=[cev, bcm])
                ring.free[bsl] = [dv]
                csp = P.op("y", lambda y, head_glob=head_glob, off=off, ln=ln, cb=cb: y.dma_start(
                    out=dr["ctx1d"][128 * head_glob:128 * (head_glob + 1), off:off + ln],
                    in_=cst[cb][:, 0:ln]), deps=[dv], dma=True)
                prev8[("cst", cb)] = csp
                prev8[("alldone",)] = dv
                ctx_spills.append(csp)
            prev8[("qfree", hl)] = pv_mms[-1]
            prev8[("kfree", hl)] = pv_mms[-1]
        prev8[("vfree",)] = pv_mms[-1]
    E["ctx1_done"] = ctx_spills[-1]
    E["attn1_done"] = ctx_spills



def phase_wo1(C, P, R, E):
    sb = C["sb"]; dr = C["dr"]; PS = C["PS"]
    SLf, ST2 = sb["SL"], sb["ST2"]
    RDENc = 16
    ring = E["ring"]
    nq = R.NQT
    halves = [(0, min(3, nq))] if nq <= 3 else [(0, (nq + 1) // 2), ((nq + 1) // 2, nq - (nq + 1) // 2)]
    ctxr = sb["PB"][:, 0:12288].rearrange("p (c n) -> p c n", c=16)
    WoT = sb["PB"][:, 12288:20480].rearrange("p (c n) -> p c n", n=512)
    p2_ops = {}
    prev_load = None
    prev_eq_mm = {}
    for (h0, hn) in halves:
        hw = hn * 128
        ld = P.op("y", lambda y, h0=h0, hw=hw: y.dma_start(
            out=ctxr[:, :, 0:hw],
            in_=dr["ctx1d"].rearrange("(c p) t -> p c t", p=128)[:, :, 128 * h0:128 * h0 + hw]),
            deps=[E["attn1_done"], prev_load, E["ln1_pb_free"]], dma=True)
        for eq in range(4):
            xb = P.op("y", lambda y, eq=eq: [
                y.dma_start_transpose(WoT[:, dc, :],
                                      dr["w1o_b"][512 * eq:512 * (eq + 1), ts(dc)])
                for dc in range(16)],
                deps=[C["cast_done"]["w1o_b"], E["attn1_done"],
                      prev_eq_mm.get("last")], dma=True, ndma=16)
            for it in range(hn):
                qt = h0 + it
                x1_t = (R.q_lo - R.l0_lo) // 128 + qt
                (sl,) = ring.take(1)
                mm = P.op("t", lambda t, it=it, sl=sl: [
                    t.matmul(PS[:, sl, :], ctxr[:, c, ts(it)], WoT[:, c, :],
                             start=(c == 0), stop=(c == 15)) for c in range(16)][-1],
                    deps=[xb, ld] + ring.free[sl])
                prev_eq_mm["last"] = mm
                ev = P.op("v", lambda v, sl=sl, qt=qt, x1_t=x1_t, eq=eq: v.tensor_tensor(
                    SLf[:, 2048 * x1_t + 512 * eq:2048 * x1_t + 512 * (eq + 1)],
                    PS[:, sl, :],
                    SLf[:, 2048 * x1_t + 512 * eq:2048 * x1_t + 512 * (eq + 1)],
                    ALU.add), deps=[mm])
                ring.free[sl] = [ev]
                p2_ops.setdefault(qt, []).append(ev)
        prev_load = prev_eq_mm["last"]
    E["p2w"] = p2_ops
    E["wo1_done"] = prev_eq_mm["last"]



def phase_ffn1(C, P, R, E):
    sb = C["sb"]; dr = C["dr"]; PS = C["PS"]; PT = C["PT"]; out = C["out"]
    SLf, XN, ST2, SM, IDT = sb["SL"], sb["XN"], sb["ST2"], sb["SMALL"], sb["IDT"]
    R1c, S2Tc = 0, 32
    ring = E["ring"]
    ST3 = sb["ST"]
    groups = []
    t0 = 0
    while t0 < R.NQT:
        n = min(3, R.NQT - t0)
        groups.append((t0, n))
        t0 += n

    xq1g = sb["PB"][:, 0:6144].rearrange("p (c n) -> p c n", c=16)
    tstr = [sb["PB"][:, 6144 + i * 1024:6144 + (i + 1) * 1024] for i in range(4)]
    x2str = [sb["PB"][:, 10240 + i * 384:10240 + (i + 1) * 384] for i in range(2)]
    stg2 = [sb["PB"][:, 11008 + i * 2048:11008 + (i + 1) * 2048]
            .rearrange("p (c n) -> p c n", n=128) for i in range(2)]
    qf = pf(C, 0, 2048)
    ostg = pf(C, 2048, 2048)
    hbuf = sb["XN"][:, :, :].rearrange("p c n -> p (c n)")

    r1s_ops = []
    for qt in range(R.NQT):
        o = P.op("v", lambda v, qt=qt: v.scalar_tensor_tensor(
            ST3[:, qt:qt + 1], ST2[:, R1c + qt:R1c + qt + 1], 1.0,
            SM[:, 8:9], ALU.mult, ALU.mult), deps=[E["m1w"], E["wo1_done"]])
        r1s_ops.append(o)

    prevg = {}
    out_stores = []
    for (g0, gn) in groups:
        xql = P.op("y", lambda y, g0=g0, gn=gn: y.dma_start(
            out=xq1g[:, :, 0:128 * gn],
            in_=dr["xq1T"].rearrange("(c p) t -> p c t", p=128)
            [:, :, 128 * g0:128 * (g0 + gn)]),
            deps=[E["xq1_spill"], E["wo1_done"], prevg.get("xql"),
                  prevg.get("w1last")], dma=True)
        hready = {}
        wave_mm = {}
        for wv in range(8):
            slots = list(range(6))
            tl = []
            for dc in range(16):
                tb = tstr[dc % 4]
                tld = P.op("y", lambda y, wv=wv, dc=dc, tb=tb: y.dma_start_transpose(
                    tb[:], dr["t1bf"][1024 * wv:1024 * (wv + 1), ts(dc)]),
                    deps=[E["tern_done"], prevg.get(("t1u", dc % 4)), xql],
                    dma=True, ndma=1)
                tl.append(tld)
                d = [tld, xql]
                for sl in slots:
                    d.extend(ring.free[sl])

                def w1f(t, dc=dc, gn=gn, tb=tb, slots=tuple(slots), first=(dc == 0),
                        last=(dc == 15)):
                    r = None
                    for ti in range(gn):
                        t.ldweights(xq1g[:, dc, ts(ti)])
                        for oc in range(2):
                            r = t.matmul(PS[:, 2 * ti + oc, :], xq1g[:, dc, ts(ti)],
                                         tb[:, 512 * oc:512 * (oc + 1)],
                                         start=first, stop=last)
                    return r
                mm = P.op("t", w1f, deps=d)
                prevg[("t1u", dc % 4)] = mm
                wave_mm[(wv, dc)] = mm
            wave_mm[(wv, "last")] = wave_mm[(wv, 15)]
            prevg["w1last"] = wave_mm[(wv, 15)]
            for ti in range(gn):
                qt = g0 + ti
                gv = P.op("s", lambda s, ti=ti, wv=wv, qt=qt: s.activation(
                    hbuf[:, 8192 * ti + 1024 * wv:8192 * ti + 1024 * (wv + 1)],
                    PS[:, :, :].rearrange("p a n -> p (a n)")[:, 1024 * ti:1024 * (ti + 1)],
                    AF.Gelu, scale=ST3[:, qt:qt + 1]),
                    deps=[wave_mm[(wv, 15)], r1s_ops[qt], E["xn1_free"]])
                for oc in range(2):
                    ring.free[2 * ti + oc] = [gv]
                prevg[("h", ti, wv)] = gv
        for ti in range(gn):
            qt = g0 + ti
            def hrow(c0, n, ti=ti):
                return hbuf[:, 8192 * ti + c0:8192 * ti + c0 + n]
            g2 = P.op("s", lambda s, ti=ti: s.activation(
                hbuf[:, 8192 * ti:8192 * (ti + 1)], hbuf[:, 8192 * ti:8192 * (ti + 1)],
                AF.Gelu), deps=[[prevg[("h", ti, w)] for w in range(8)]])
            if DEBUG_PHASE == 6 and g0 == 0 and ti == 0:
                base = 1536 if R.pre_n == 0 else 0
                dprev = g2
                for blk in range(4):
                    dc_ = P.op("v", lambda v, blk=blk: v.tensor_copy(
                        ostg[:], hbuf[:, 2048 * blk:2048 * (blk + 1)]), deps=[dprev])
                    dst_ = P.op("y", lambda y, blk=blk, base=base: y.dma_start(
                        out=C["out"][base + 128 * blk:base + 128 * (blk + 1), :],
                        in_=ostg[:]), deps=[dc_], dma=True)
                    dprev = dst_
                prevg["ost"] = dprev
            dbh2 = None
            am = P.op("v", lambda v, ti=ti: v.reduce_max(
                SM[:, 5:6], hbuf[:, 8192 * ti:8192 * (ti + 1)], AX.X,
                apply_absolute_value=True), deps=[g2])
            sq_ps = []
            for p_ in range(4):
                o = P.op("v", lambda v, ti=ti, p_=p_: v.scalar_tensor_tensor(
                    qf[:], hrow(2048 * p_, 2048, ti), 1.0, hrow(2048 * p_, 2048, ti),
                    ALU.mult, ALU.mult, accum_out=SM[:, p_:p_ + 1]), deps=[g2])
                sq_ps.append(o)
            t15 = P.op("v", lambda v: v.tensor_scalar(
                SM[:, 15:16], SM[:, 5:6], SM[:, 9:10], None, ALU.mult),
                deps=[E["m2w"]])
            ack = P.op("s", lambda s: s.nop(), deps=[sq_ps[3], t15])
            t1_ = P.op("v", lambda v: v.tensor_tensor(
                SM[:, 13:14], SM[:, 0:1], SM[:, 1:2], ALU.add), deps=[ack])
            t2_ = P.op("v", lambda v: v.tensor_tensor(
                SM[:, 6:7], SM[:, 2:3], SM[:, 3:4], ALU.add))
            akb = P.op("s", lambda s: s.nop(), deps=[t1_, t2_])
            t3_ = P.op("v", lambda v: v.tensor_tensor(
                SM[:, 13:14], SM[:, 13:14], SM[:, 6:7], ALU.add), deps=[akb])
            akc = P.op("s", lambda s: s.nop(), deps=[t3_])
            rq = P.op("v", lambda v: v.reciprocal(SM[:, 7:8], SM[:, 13:14]), deps=[akc])
            sq2 = P.op("s", lambda s: s.activation(SM[:, 7:8], SM[:, 7:8], AF.Sqrt),
                       deps=[rq])
            s2t = P.op("v", lambda v, qt=qt: v.scalar_tensor_tensor(
                ST2[:, S2Tc + qt:S2Tc + qt + 1], SM[:, 15:16],
                float(np.sqrt(I1) / 127.0), SM[:, 7:8], ALU.mult, ALU.mult),
                deps=[sq2])
            s2t2 = s2t
            tc2 = P.op("v", lambda v: v.tensor_scalar(
                SM[:, 12:13], SM[:, 5:6], 1.0 / 127.0, None, ALU.mult), deps=[sq2])
            ak2 = P.op("s", lambda s: s.nop(), deps=[tc2])
            c2r = P.op("v", lambda v: v.reciprocal(SM[:, 12:13], SM[:, 12:13]),
                       deps=[ak2])
            c2 = P.op("s", lambda s: s.nop(), deps=[c2r])
            if DEBUG_PHASE == 6 and g0 == 0 and ti == 0:
                base = 1536 if R.pre_n == 0 else 0
                dc1 = P.op("v", lambda v: v.tensor_copy(ostg[:, 0:16], SM[:]),
                           deps=[s2t, c2r])
                dc2 = P.op("v", lambda v: v.tensor_copy(ostg[:, 16:80], ST2[:, 0:64]),
                           deps=[dc1])
                dst2 = P.op("y", lambda y, base=base: y.dma_start(
                    out=C["out"][base + 384:base + 512, 0:80], in_=ostg[:, 0:80]),
                    deps=[dc2], dma=True)
                prevg["ost"] = dst2
            spills = []
            for p_ in range(4):
                q1 = P.op("v", lambda v, p_=p_, ti=ti: v.tensor_scalar(
                    qf[:], hrow(2048 * p_, 2048, ti), SM[:, 12:13], MAGIC,
                    ALU.mult, ALU.add), deps=[c2, sq_ps if p_ == 0 else None])
                q2 = P.op("v", lambda v, p_=p_, ti=ti: v.tensor_scalar(
                    hrow(2048 * p_, 2048, ti), qf[:], MAGIC, 127.0,
                    ALU.subtract, ALU.min))
                q3 = P.op("v", lambda v, p_=p_, ti=ti: v.tensor_scalar(
                    hrow(2048 * p_, 2048, ti), hrow(2048 * p_, 2048, ti), -128.0, None,
                    ALU.max))
                sbf = p_ % 2
                tq = P.op("t", lambda t, p_=p_, sbf=sbf, ti=ti: [
                    t.transpose(PT[:, k % 8, :], hrow(2048 * p_ + 128 * k, 128, ti), IDT[:])
                    for k in range(8)][-1],
                    deps=[q3, prevg.get("xq2ev"), spills[p_ - 2] if p_ >= 2 else
                          prevg.get("lastspill")])
                ev1 = P.op("v", lambda v, sbf=sbf: [
                    v.tensor_copy(stg2[sbf][:, k, :], PT[:, k % 8, :])
                    for k in range(8)][-1], deps=[tq])
                tq2 = P.op("t", lambda t, p_=p_, ti=ti: [
                    t.transpose(PT[:, k % 8, :], hrow(2048 * p_ + 128 * (8 + k), 128, ti), IDT[:])
                    for k in range(8)][-1], deps=[ev1])
                ev2 = P.op("v", lambda v, sbf=sbf: [
                    v.tensor_copy(stg2[sbf][:, 8 + k, :], PT[:, k % 8, :])
                    for k in range(8)][-1], deps=[tq2])
                prevg["xq2ev"] = ev2
                prevg["lastspill"] = None
                sp = P.op("y", lambda y, p_=p_, sbf=sbf: y.dma_start(
                    out=dr["xq2T"][2048 * p_:2048 * (p_ + 1), 0:128]
                    .rearrange("(c p) t -> p c t", p=128),
                    in_=stg2[sbf][:, :, :]), deps=[ev2], dma=True)
                spills.append(sp)
                prevg["lastspill"] = sp
            ostores = []
            for half in range(2):
                ssl = ring.take(2)
                d2 = []
                for sl in ssl:
                    d2.extend(ring.free[sl])
                mms2 = []
                for i2c in range(64):
                    tb = tstr[i2c % 4]
                    t2l = P.op("y", lambda y, i2c=i2c, half=half, tb=tb: y.dma_start_transpose(
                        tb[:], dr["t2bf"][1024 * half:1024 * (half + 1), ts(i2c)]),
                        deps=[E["tern_done"], prevg.get(("t1u", i2c % 4)),
                              mms2[i2c - 4] if i2c >= 4 else None], dma=True, ndma=1)
                    xl2 = P.op("y", lambda y, i2c=i2c, b=i2c % 2: y.dma_start(
                        out=x2str[b][:, 0:128],
                        in_=dr["xq2T"][128 * i2c:128 * (i2c + 1), 0:128]),
                        deps=[spills[-1], mms2[i2c - 2] if i2c >= 2 else None], dma=True)

                    def w2f(t, i2c=i2c, b=i2c % 2, ssl=tuple(ssl), tb=tb,
                            first=(i2c == 0), last=(i2c == 63)):
                        t.ldweights(x2str[b][:, 0:128])
                        r = None
                        for oc in range(2):
                            r = t.matmul(PS[:, ssl[oc], :], x2str[b][:, 0:128],
                                         tb[:, 512 * oc:512 * (oc + 1)],
                                         start=first, stop=last)
                        return r
                    mm2 = P.op("t", w2f, deps=[t2l, xl2] + (d2 if i2c == 0 else []))
                    prevg[("t1u", i2c % 4)] = mm2
                    mms2.append(mm2)
                x1_t = (R.q_lo - R.l0_lo) // 128 + qt
                for oc in range(2):
                    cols = 1024 * half + 512 * oc
                    fe = P.op("v", lambda v, sl=ssl[oc], cols=cols, x1_t=x1_t, qt=qt: v.scalar_tensor_tensor(
                        ostg[:, cols:cols + 512], PS[:, sl, :],
                        ST2[:, S2Tc + qt:S2Tc + qt + 1],
                        SLf[:, 2048 * x1_t + cols:2048 * x1_t + cols + 512],
                        ALU.mult, ALU.add),
                        deps=[mms2[-1], s2t2, E["p2w"][qt], prevg.get("ost")])
                    ring.free[ssl[oc]] = [fe]
                    ostores.append(fe)
            ost = P.op("y", lambda y, qt=qt: y.dma_start(
                out=out[R.q_lo + 128 * qt:R.q_lo + 128 * (qt + 1), :], in_=ostg[:]),
                deps=[ostores], dma=True)
            out_stores.append(ost)
            prevg["ost"] = ost
        prevg["xql"] = xql
    P.op("v", lambda v: v.nop(), deps=[out_stores[-1]])



def debug_out_rows(C, P, R, E, dep_ops, which):
    sb = C["sb"]; dr = C["dr"]; out = C["out"]
    bstg = [pb(C, 0, 2048), pb(C, 2048, 2048)]
    fstg = [pf(C, 0, 2048), pf(C, 2048, 2048)]
    alldone = dep_ops[-1]
    sts = []
    cps = []
    for t_i in range(R.NT0):
        b = t_i % 2
        ld = P.op("y", lambda y, t_i=t_i, b=b: y.dma_start(
            out=bstg[b][:], in_=dr["p1sp"][128 * t_i:128 * (t_i + 1), :]),
            deps=[alldone, cps[t_i - 2] if t_i >= 2 else None], dma=True)
        cp = P.op("v", lambda v, b=b: v.tensor_copy(fstg[b][:], bstg[b][:]),
                  deps=[ld, sts[t_i - 2] if t_i >= 2 else None])
        cps.append(cp)
        st = P.op("y", lambda y, t_i=t_i, b=b: y.dma_start(
            out=out[R.l0_lo + 128 * t_i:R.l0_lo + 128 * (t_i + 1), :],
            in_=fstg[b][:]), deps=[cp], dma=True)
        sts.append(st)
    P.op("v", lambda v: v.nop(), deps=[sts[-1], E["mean_ops"][-1]])



def stage_inputs(inputs):
    f = np.float32
    x = np.ascontiguousarray(np.asarray(inputs["L0_attn_x"], f))
    bias_pp = np.ascontiguousarray(np.stack(
        [np.asarray(inputs["L0_attn_b" + k], f).reshape(16, 128).T for k in "qkv"], axis=1))
    scl_pp = np.ascontiguousarray(np.stack(
        [np.asarray(inputs["L0_ffn_" + k + "_scale"], f).reshape(48, 128).T
         for k in ("gate", "up")], axis=1))
    bo_bc = np.ascontiguousarray(np.broadcast_to(
        np.asarray(inputs["L0_attn_bo"], f), (128, D))).astype(ml_dtypes.bfloat16)
    ds_bc = np.ascontiguousarray(np.broadcast_to(
        np.asarray(inputs["L0_ffn_down_scale"], f), (128, D))).astype(ml_dtypes.bfloat16)
    ident = np.eye(128, dtype=ml_dtypes.bfloat16)
    masks = np.zeros((12, 128, 512), np.float32)
    jj = np.arange(128)[:, None]; ii = np.arange(128)[None, :]
    for r0 in range(-3, 9):
        for b in range(4):
            r = r0 + b
            if 1 <= r <= 7:
                masks[r0 + 3, :, b * 128:(b + 1) * 128] = 1.0
            elif r == 0:
                masks[r0 + 3, :, b * 128:(b + 1) * 128] = (jj <= ii)
            elif r == 8:
                masks[r0 + 3, :, b * 128:(b + 1) * 128] = (jj > ii)
    com = dict(
        w0q=inputs["L0_attn_Wq"], w0k=inputs["L0_attn_Wk"], w0v=inputs["L0_attn_Wv"],
        w0o=inputs["L0_attn_Wo"], w1q=inputs["L1_attn_Wq"], w1k=inputs["L1_attn_Wk"],
        w1v=inputs["L1_attn_Wv"], w1o=inputs["L1_attn_Wo"], gw=inputs["L0_ffn_gate_w"],
        uw=inputs["L0_ffn_up_w"], dw=inputs["L0_ffn_down_w"], fw1=inputs["L1_ffn_w1"],
        fw2=inputs["L1_ffn_w2"])
    com = {k: np.ascontiguousarray(np.asarray(v, f)) for k, v in com.items()}
    com.update(ident=ident, bias_pp=bias_pp, scl_pp=scl_pp, bo_bc=bo_bc,
               ds_bc=ds_bc, masks=np.ascontiguousarray(masks.astype(ml_dtypes.bfloat16)))
    in_maps = []
    for c in range(8):
        m = dict(com)
        m["x"] = np.ascontiguousarray(x[c // 2])
        in_maps.append(m)
    return in_maps


def kernel(**inputs):
    in_maps = stage_inputs(inputs)
    nc = build_program()
    res = run_bass_kernel_spmd(nc, in_maps, core_ids=list(range(8)))
    outs = np.empty((NB, S, D), np.float32)
    for seq in range(NB):
        outs[seq, :T1] = res.results[2 * seq]["out"][:T1]
        outs[seq, T1:] = res.results[2 * seq + 1]["out"][T1:]
    return outs
